# version 01
# baseline (speedup 1.0000x reference)
import numpy as np
import ml_dtypes
from contextlib import ExitStack

import concourse.bass as bass
import concourse.mybir as mybir
from concourse.bass import IndirectOffsetOnAxis
from concourse.bass_utils import run_bass_kernel_spmd

F32 = mybir.dt.float32
BF16 = mybir.dt.bfloat16
I32 = mybir.dt.int32
AF = mybir.ActivationFunctionType
ALU = mybir.AluOpType
AX = mybir.AxisListType
BF = ml_dtypes.bfloat16

NCORES = 8
T = 1024
D = 4096
HD = 128
NH = 32
NKV = 8
KVD = NKV * HD
S = 4096
NEG = -1.0e30
EPS = 1e-5

cc = np.ascontiguousarray


def build_prep0(stage=3, nqk=40):
    nc = bass.Bass("TRN2", target_bir_lowering=False)
    X = nc.dram_tensor("x", [T, D], F32, kind="ExternalInput").ap()
    WQ = nc.dram_tensor("wq_t", [NH, D, HD], F32, kind="ExternalInput").ap()
    WK = nc.dram_tensor("wk_t", [NKV, D, HD], F32, kind="ExternalInput").ap()
    WV = nc.dram_tensor("wv", [D, KVD], F32, kind="ExternalInput").ap()
    CQ = nc.dram_tensor("cq", [HD, T], F32, kind="ExternalInput").ap()
    SQ = nc.dram_tensor("sq", [HD, T], F32, kind="ExternalInput").ap()
    CK = nc.dram_tensor("ck", [HD, T], F32, kind="ExternalInput").ap()
    SK = nc.dram_tensor("sk", [HD, T], F32, kind="ExternalInput").ap()
    EYE = nc.dram_tensor("eye", [128, 128], F32, kind="ExternalInput").ap()
    XNT = nc.dram_tensor("xnT", [D, T], BF16, kind="ExternalOutput").ap()
    QT = nc.dram_tensor("qT", [D, T], BF16, kind="ExternalOutput").ap()
    KT = nc.dram_tensor("kT", [KVD, T], BF16, kind="ExternalOutput").ap()
    VO = nc.dram_tensor("v", [T, KVD], BF16, kind="ExternalOutput").ap()

    qk_proj = ([("q", h) for h in range(NH)] + [("k", g) for g in range(NKV)])[:nqk]
    NQK = len(qk_proj)
    NVG = 32

    with ExitStack() as st:
        sb = lambda *a: st.enter_context(nc.sbuf_tensor(*a))
        ps = lambda *a: st.enter_context(nc.psum_tensor(*a))
        xt = [sb(f"xt{i}", [128, D], F32) for i in range(2)]
        junk = sb("junk", [128, D], BF16)
        stat = sb("stat", [128, 8], F32)
        xn = sb("xn", [128, D], BF16)
        xnT = sb("xnTs", [128, 32, T], BF16)
        eye = sb("eyeb", [128, 128], BF16)
        cqs = sb("cqs", [128, T], F32)
        sqs = sb("sqs", [128, T], F32)
        cks = sb("cks", [128, T], F32)
        sks = sb("sks", [128, T], F32)
        wbuf = [sb(f"wbuf{i}", [128, 32, 128], BF16) for i in range(2)]
        wvb = [sb(f"wvb{i}", [128, 32, 256], BF16) for i in range(2)]
        rt1 = sb("rt1", [128, 512], F32)
        rt2 = sb("rt2", [128, 512], F32)
        ro = [sb(f"ro{i}", [128, 512], BF16) for i in range(2)]
        vout = [sb(f"vout{i}", [128, 256], BF16) for i in range(2)]
        ptp = [ps(f"ptp{i}", [128, 128], BF16) for i in range(2)]
        pmm = [ps(f"pmm{i}", [128, 512], F32) for i in range(2)]
        lt = st.enter_context(nc.semaphore("lt"))
        lx = [st.enter_context(nc.semaphore(f"lx{i}")) for i in range(2)]
        ley = st.enter_context(nc.semaphore("ley"))
        lwq = [st.enter_context(nc.semaphore(f"lwq{i}")) for i in range(2)]
        lwv = [st.enter_context(nc.semaphore(f"lwv{i}")) for i in range(2)]
        sxn = st.enter_context(nc.semaphore("sxn"))
        sqk = [st.enter_context(nc.semaphore(f"sqk{i}")) for i in range(2)]
        svv = [st.enter_context(nc.semaphore(f"svv{i}")) for i in range(2)]
        s_ln = st.enter_context(nc.semaphore("s_ln"))
        v_ln = st.enter_context(nc.semaphore("v_ln"))
        tp = st.enter_context(nc.semaphore("tp"))
        tc = st.enter_context(nc.semaphore("tc"))
        mm = st.enter_context(nc.semaphore("mm"))
        ev = st.enter_context(nc.semaphore("ev"))
        so = st.enter_context(nc.semaphore("so"))
        block = st.enter_context(nc.Block())

        @block.sync
        def _(sy):
            sy.dma_start(out=cqs[:], in_=CQ).then_inc(lt, 16)
            sy.dma_start(out=sqs[:], in_=SQ).then_inc(lt, 16)
            sy.dma_start(out=cks[:], in_=CK).then_inc(lt, 16)
            sy.dma_start(out=sks[:], in_=SK).then_inc(lt, 16)
            for i in range(8):
                if i >= 2:
                    sy.wait_ge(s_ln, 4 * i - 4)
                    sy.wait_ge(v_ln, 7 * i - 13)
                sy.dma_start(out=xt[i % 2][:], in_=X[128 * i:128 * (i + 1), :]).then_inc(lx[i % 2], 16)
            sy.wait_ge(tc, 256)
            for j in range(32):
                sy.dma_start(out=XNT[128 * j:128 * (j + 1), :], in_=xnT[:, j, :]).then_inc(sxn, 16)
            if stage < 2:
                sy.wait_ge(so, 16 * 32)
                return
            for G in range(2 * NQK):
                pi, th = G // 2, G % 2
                kind, idx = qk_proj[pi]
                sy.wait_ge(ev, G + 1)
                dst = QT if kind == "q" else KT
                sy.dma_start(out=dst[128 * idx:128 * (idx + 1), 512 * th:512 * (th + 1)],
                             in_=ro[G % 2][:]).then_inc(sqk[G % 2], 16)
            if stage < 3:
                sy.wait_ge(so, 16 * (32 + 2 * NQK))
                return
            for h in range(NVG):
                hf, tt = h // 8, h % 8
                sy.wait_ge(ev, 2 * NQK + h + 1)
                sy.dma_start(out=VO[128 * tt:128 * (tt + 1), 256 * hf:256 * (hf + 1)],
                             in_=vout[h % 2][:]).then_inc(svv[h % 2], 16)
            sy.wait_ge(sxn, 16 * 32)
            sy.wait_ge(sqk[0], 16 * NQK)
            sy.wait_ge(sqk[1], 16 * NQK)
            sy.wait_ge(svv[0], 16 * (NVG // 2))
            sy.wait_ge(svv[1], 16 * (NVG // 2))

        @block.gpsimd
        def _(g):
            g.dma_start(out=eye[:], in_=EYE).then_inc(ley, 16)
            if stage < 2:
                return
            for pi, (kind, idx) in enumerate(qk_proj):
                if pi >= 2:
                    g.wait_ge(mm, 2 * pi - 2)
                src = WQ[idx] if kind == "q" else WK[idx]
                g.dma_start(out=wbuf[pi % 2][:],
                            in_=src.rearrange("(j p) n -> p j n", p=128)).then_inc(lwq[pi % 2], 16)
            if stage < 3:
                return
            for hf in range(4):
                if hf >= 2:
                    g.wait_ge(mm, 2 * NQK + 8 * (hf - 1))
                g.dma_start(out=wvb[hf % 2][:],
                            in_=WV[:, 256 * hf:256 * (hf + 1)].rearrange("(j p) n -> p j n", p=128)
                            ).then_inc(lwv[hf % 2], 16)

        @block.scalar
        def _(s):
            for i in range(8):
                s.wait_ge(lx[i % 2], 16 * (i // 2 + 1))
                s.activation(junk[:], xt[i % 2][:], AF.Square,
                             accum_out=stat[:, 1:2]).then_inc(s_ln, 1)
                s.wait_ge(v_ln, 7 * i + 5)
                s.activation(stat[:, 3:4], stat[:, 4:5], AF.Ln).then_inc(s_ln, 1)
                s.wait_ge(v_ln, 7 * i + 6)
                s.activation(stat[:, 5:6], stat[:, 4:5], AF.Exp).then_inc(s_ln, 1)
                s.wait_ge(v_ln, 7 * i + 7)
                s.wait_ge(s_ln, 4 * i + 3)
                if i > 0:
                    s.wait_ge(tp, 32 * i)
                s.activation(xn[:], xt[i % 2][:], AF.Identity, scale=stat[:, 5:6],
                             bias=stat[:, 6:7]).then_inc(s_ln, 1)
            if stage < 3:
                return
            for h in range(NVG):
                s.wait_ge(mm, 2 * NQK + h + 1)
                if h >= 2:
                    s.wait_ge(svv[h % 2], 16 * (h // 2))
                s.activation(vout[h % 2][:], pmm[(2 * NQK + h) % 2][:, 0:256], AF.Copy).then_inc(ev, 1)

        @block.vector
        def _(v):
            for i in range(8):
                v.wait_ge(s_ln, 4 * i + 1)
                v.wait_ge(lx[i % 2], 16 * (i // 2 + 1))
                v.reduce_sum(stat[:, 0:1], xt[i % 2][:], axis=AX.X).then_inc(v_ln, 1)
                v.wait_ge(v_ln, 7 * i + 1)
                v.tensor_scalar(stat[:, 2:3], stat[:, 0:1], 1.0 / D, None, ALU.mult).then_inc(v_ln, 1)
                v.wait_ge(v_ln, 7 * i + 2)
                v.tensor_tensor(stat[:, 7:8], stat[:, 2:3], stat[:, 2:3], ALU.mult).then_inc(v_ln, 1)
                v.wait_ge(v_ln, 7 * i + 3)
                v.scalar_tensor_tensor(stat[:, 3:4], stat[:, 1:2], 1.0 / D, stat[:, 7:8],
                                       ALU.mult, ALU.subtract).then_inc(v_ln, 1)
                v.wait_ge(v_ln, 7 * i + 4)
                v.tensor_scalar(stat[:, 4:5], stat[:, 3:4], EPS, None,
                                ALU.add).then_inc(v_ln, 1)
                v.wait_ge(s_ln, 4 * i + 2)
                v.tensor_scalar(stat[:, 4:5], stat[:, 3:4], -0.5, None,
                                ALU.mult).then_inc(v_ln, 1)
                v.wait_ge(s_ln, 4 * i + 3)
                v.scalar_tensor_tensor(stat[:, 6:7], stat[:, 2:3], -1.0, stat[:, 5:6],
                                       ALU.mult, ALU.mult).then_inc(v_ln, 1)
                for j in range(32):
                    n = 32 * i + j
                    v.wait_ge(tp, n + 1)
                    v.tensor_copy(xnT[:, j, 128 * i:128 * (i + 1)], ptp[n % 2][:]).then_inc(tc, 1)
            if stage < 2:
                return
            v.wait_ge(lt, 64)
            for G in range(2 * NQK):
                pi, th = G // 2, G % 2
                kind, _ = qk_proj[pi]
                Ct = cqs if kind == "q" else cks
                St = sqs if kind == "q" else sks
                sl = slice(512 * th, 512 * (th + 1))
                p = pmm[G % 2]
                v.wait_ge(mm, G + 1)
                if G >= 2:
                    v.wait_ge(sqk[G % 2], 16 * (G // 2))
                if G >= 1:
                    v.wait_ge(ev, G)
                vb = 56 + 3 * G
                v.tensor_tensor(rt1[:], p[:], Ct[:, sl], ALU.mult).then_inc(v_ln, 1)
                v.tensor_tensor(rt2[0:64, :], p[64:128, :], St[0:64, sl], ALU.mult).then_inc(v_ln, 1)
                v.tensor_tensor(rt2[64:128, :], p[0:64, :], St[64:128, sl], ALU.mult).then_inc(v_ln, 1)
                v.wait_ge(v_ln, vb + 3)
                v.tensor_tensor(ro[G % 2][:], rt1[:], rt2[:], ALU.add).then_inc(ev, 1)

        @block.tensor
        def _(t):
            t.wait_ge(ley, 16)
            for i in range(8):
                t.wait_ge(s_ln, 4 * i + 4)
                for j in range(32):
                    n = 32 * i + j
                    if n >= 2:
                        t.wait_ge(tc, n - 1)
                    t.transpose(ptp[n % 2][:], xn[:, 128 * j:128 * (j + 1)], eye[:]).then_inc(tp, 1)
            t.wait_ge(tc, 256)
            if stage < 2:
                return
            for G in range(2 * NQK):
                pi, th = G // 2, G % 2
                if th == 0:
                    t.wait_ge(lwq[pi % 2], 16 * (pi // 2 + 1))
                if G >= 2:
                    t.wait_ge(ev, G - 1)
                for j in range(32):
                    inst = t.matmul(pmm[G % 2][:], wbuf[pi % 2][:, j, :],
                                    xnT[:, j, 512 * th:512 * (th + 1)],
                                    start=(j == 0), stop=(j == 31))
                inst.then_inc(mm, 1)
            if stage < 3:
                return
            for h in range(NVG):
                hf, tt = h // 8, h % 8
                G = 2 * NQK + h
                if tt == 0:
                    t.wait_ge(lwv[hf % 2], 16 * (hf // 2 + 1))
                t.wait_ge(ev, G - 1)
                for j in range(32):
                    inst = t.matmul(pmm[G % 2][:, 0:256], xnT[:, j, 128 * tt:128 * (tt + 1)],
                                    wvb[hf % 2][:, j, :], start=(j == 0), stop=(j == 31))
                inst.then_inc(mm, 1)
    return nc


def host_prep0_inputs(x_full2d, wq, wk, wv):
    perm_d = np.concatenate([np.arange(0, HD, 2), np.arange(1, HD, 2)])
    colp_q = np.concatenate([HD * h + perm_d for h in range(NH)])
    colp_k = np.concatenate([HD * g + perm_d for g in range(NKV)])
    wq_t = cc(wq[:, colp_q].reshape(D, NH, HD).transpose(1, 0, 2))
    wk_t = cc(wk[:, colp_k].reshape(D, NKV, HD).transpose(1, 0, 2))
    wv_c = cc(wv)
    eye = np.eye(128, dtype=np.float32)
    inv = 10000.0 ** (-np.arange(64) / 64.0)
    maps = []
    for c in range(NCORES):
        pos = np.arange(1024 * (c % 4), 1024 * (c % 4) + 1024, dtype=np.float64)
        ang = pos[:, None] * inv[None, :]
        cos = np.cos(ang).astype(np.float32)
        sin = np.sin(ang).astype(np.float32)
        Ct = cc(np.concatenate([cos, cos], 1).T)
        St = cc(np.concatenate([-sin, sin], 1).T)
        sc = np.float32(1.0 / np.sqrt(HD))
        maps.append(dict(
            x=cc(x_full2d[T * c:T * (c + 1)]),
            wq_t=wq_t, wk_t=wk_t, wv=wv_c,
            cq=Ct * sc, sq=St * sc, ck=Ct, sk=St, eye=eye,
        ))
    return maps




def build_attn0():
    nc = bass.Bass("TRN2", target_bir_lowering=False)
    QTF = nc.dram_tensor("qTf", [128, 8, S], BF16, kind="ExternalInput").ap()
    KTF = nc.dram_tensor("kTf", [128, 2, S], BF16, kind="ExternalInput").ap()
    VA = nc.dram_tensor("vA", [128, 32, 2, 129], BF16, kind="ExternalInput").ap()
    MSK = nc.dram_tensor("msk", [128, 4, 512], F32, kind="ExternalInput").ap()
    CTX = nc.dram_tensor("ctx", [128, 32, 8, 128], BF16, kind="ExternalOutput").ap()

    tiles = []
    for h in range(8):
        for qc in range(8):
            nkt = 4 * qc + 4
            for kt in range(nkt):
                tiles.append((h, qc, kt, nkt, h * 8 + qc))
    NT = len(tiles)
    cumtiles = {}
    n = 0
    for h in range(8):
        for qc in range(8):
            n += 4 * qc + 4
            cumtiles[h * 8 + qc] = n
    masked_idx = {}
    mi = 0
    for n_, (h, qc, kt, nkt, g) in enumerate(tiles):
        if kt >= nkt - 4:
            masked_idx[n_] = mi
            mi += 1
    NMSK = mi

    with ExitStack() as st:
        sb = lambda *a: st.enter_context(nc.sbuf_tensor(*a))
        ps = lambda *a: st.enter_context(nc.psum_tensor(*a))
        qT = sb("qTs", [128, 8, S], BF16)
        kT = sb("kTs", [128, 2, S], BF16)
        vA = sb("vAs", [128, 32, 2, 129], BF16)
        msk = sb("msks", [128, 4, 512], F32)
        expb = [sb(f"expb{i}", [128, 512], BF16) for i in range(2)]
        sxp = [sb(f"sxp{i}", [128, 512], F32) for i in range(2)]
        rcp = sb("rcp", [128, 4], F32)
        ctx = sb("ctxs", [128, 32, 8, 128], BF16)
        scp = [ps(f"scp{i}", [128, 512], F32) for i in range(2)]
        ctp = [ps(f"ctp{i}", [128, 129], F32) for i in range(4)]
        lds = st.enter_context(nc.semaphore("lds"))
        mm = st.enter_context(nc.semaphore("mm"))
        avm = st.enter_context(nc.semaphore("avm"))
        exv = st.enter_context(nc.semaphore("exv"))
        vx = st.enter_context(nc.semaphore("vx"))
        vrc = st.enter_context(nc.semaphore("vrc"))
        ctev = st.enter_context(nc.semaphore("ctev"))
        so = st.enter_context(nc.semaphore("so"))
        block = st.enter_context(nc.Block())

        @block.sync
        def _(sy):
            sy.dma_start(out=qT[:], in_=QTF).then_inc(lds, 16)
            sy.dma_start(out=kT[:], in_=KTF).then_inc(lds, 16)
            sy.dma_start(out=vA[:], in_=VA).then_inc(lds, 16)
            sy.dma_start(out=msk[:], in_=MSK).then_inc(lds, 16)
            sy.wait_ge(ctev, 64)
            sy.dma_start(out=CTX, in_=ctx[:]).then_inc(so, 16)
            sy.wait_ge(so, 16)

        @block.tensor
        def _(t):
            t.wait_ge(lds, 64)
            for n_, (h, qc, kt, nkt, g) in enumerate(tiles):
                if n_ >= 2:
                    t.wait_ge(exv, n_ - 1)
                t.matmul(scp[n_ % 2][:], kT[:, h // 4, 128 * kt:128 * (kt + 1)],
                         qT[:, h, 512 * qc:512 * (qc + 1)],
                         start=True, stop=True).then_inc(mm, 1)
                if n_ >= 1:
                    _emit_av(t, tiles[n_ - 1], n_ - 1, expb, vA, ctp, exv, ctev, avm)
            _emit_av(t, tiles[NT - 1], NT - 1, expb, vA, ctp, exv, ctev, avm)

        @block.vector
        def _(v):
            nv = 0
            for n_, (h, qc, kt, nkt, g) in enumerate(tiles):
                if kt >= nkt - 4:
                    r = kt - 4 * qc
                    v.wait_ge(mm, n_ + 1)
                    if n_ >= 2:
                        v.wait_ge(exv, n_ - 1)
                    v.tensor_tensor(sxp[n_ % 2][:], scp[n_ % 2][:], msk[:, r, :],
                                    ALU.add).then_inc(vx, 1)
                if kt == nkt - 1:
                    v.wait_ge(avm, cumtiles[g])
                    if g >= 1:
                        v.wait_ge(ctev, g)
                    for qs4 in range(4):
                        v.reciprocal(rcp[:, qs4:qs4 + 1],
                                     ctp[qs4][:, 128:129]).then_inc(vrc, 1)

        @block.scalar
        def _(s):
            for n_, (h, qc, kt, nkt, g) in enumerate(tiles):
                if n_ >= 2:
                    s.wait_ge(avm, n_ - 1)
                if kt >= nkt - 4:
                    s.wait_ge(vx, masked_idx[n_] + 1)
                    s.activation(expb[n_ % 2][:], sxp[n_ % 2][:], AF.Exp).then_inc(exv, 1)
                else:
                    s.wait_ge(mm, n_ + 1)
                    s.activation(expb[n_ % 2][:], scp[n_ % 2][:], AF.Exp).then_inc(exv, 1)
                if kt == nkt - 1:
                    s.wait_ge(vrc, 4 * (g + 1))
                    qt_global = 4 * qc
                    for qs4 in range(4):
                        inst = s.activation(ctx[:, qt_global + qs4, h, :],
                                            ctp[qs4][:, 0:128], AF.Copy,
                                            scale=rcp[:, qs4:qs4 + 1])
                    inst.then_inc(ctev, 1)

    return nc


def _emit_av(t, tile, m, expb, vA, ctp, exv, ctev, avm):
    h, qc, kt, nkt, g = tile
    t.wait_ge(exv, m + 1)
    if kt == 0 and g >= 1:
        t.wait_ge(ctev, g)
    for qs4 in range(4):
        inst = t.matmul(ctp[qs4][:], expb[m % 2][:, 128 * qs4:128 * (qs4 + 1)],
                        vA[:, kt, h // 4, :], start=(kt == 0), stop=(kt == nkt - 1))
    inst.then_inc(avm, 1)


def host_attn0_inputs(qT_cores, kT_cores, v_cores):
    maps = []
    qT_b = [np.concatenate(qT_cores[4 * b:4 * b + 4], axis=1) for b in range(2)]
    kT_b = [np.concatenate(kT_cores[4 * b:4 * b + 4], axis=1) for b in range(2)]
    v_b = [np.concatenate(v_cores[4 * b:4 * b + 4], axis=0) for b in range(2)]
    kl = np.arange(128)[:, None]
    ql = np.arange(512)[None, :]
    m4 = np.stack([np.where(128 * r + kl > ql, NEG, 0.0) for r in range(4)], 0).astype(np.float32)
    msk = cc(m4.transpose(1, 0, 2))
    for i in range(NCORES):
        b, m = i // 4, i % 4
        qTf = cc(qT_b[b][128 * 8 * m:128 * 8 * (m + 1)].reshape(8, 128, S).transpose(1, 0, 2))
        kTf = cc(kT_b[b][256 * m:256 * (m + 1)].reshape(2, 128, S).transpose(1, 0, 2))
        vA = np.ones((128, 32, 2, 129), dtype=BF)
        for gl in range(2):
            vg = v_b[b][:, 128 * (2 * m + gl):128 * (2 * m + gl + 1)]
            vA[:, :, gl, 0:128] = vg.reshape(32, 128, 128).transpose(1, 0, 2)
        maps.append(dict(qTf=qTf, kTf=kTf, vA=cc(vA), msk=msk))
    return maps


def host_attn0_ctx(results):
    ctx_b = [np.zeros((4096, 4096), dtype=BF) for _ in range(2)]
    for i in range(NCORES):
        b, m = i // 4, i % 4
        arr = np.asarray(results[i]["ctx"])
        ctx_b[b][:, 1024 * m:1024 * (m + 1)] = arr.transpose(1, 0, 2, 3).reshape(4096, 1024)
    return ctx_b


ARO = [0, 4, 1, 5, 2, 6, 3, 7]


def lperm_global(ncol_percore):
    return np.array([8 * (ncol_percore * c + col) + p
                     for c in range(8) for p in range(8) for col in range(ncol_percore)])


def build_dequant():
    nc = bass.Bass("TRN2", target_bir_lowering=False)
    GQ = nc.dram_tensor("gq", [4096, 172], I32, kind="ExternalInput").ap()
    GS = nc.dram_tensor("gs", [32, 1376], F32, kind="ExternalInput").ap()
    GZ = nc.dram_tensor("gz", [32, 172], I32, kind="ExternalInput").ap()
    UQ = nc.dram_tensor("uq", [4096, 172], I32, kind="ExternalInput").ap()
    US = nc.dram_tensor("us", [32, 1376], F32, kind="ExternalInput").ap()
    UZ = nc.dram_tensor("uz", [32, 172], I32, kind="ExternalInput").ap()
    DQ = nc.dram_tensor("dq", [11008, 64], I32, kind="ExternalInput").ap()
    DSE = nc.dram_tensor("dse", [11008, 512], F32, kind="ExternalInput").ap()
    DZE = nc.dram_tensor("dze", [11008, 64], I32, kind="ExternalInput").ap()
    ONE = nc.dram_tensor("one", [1, 128], F32, kind="ExternalInput").ap()
    WG = nc.dram_tensor("Wg", [4096, 1376], BF16, kind="ExternalOutput").ap()
    WU = nc.dram_tensor("Wu", [4096, 1376], BF16, kind="ExternalOutput").ap()
    WD = nc.dram_tensor("Wd", [11008, 512], BF16, kind="ExternalOutput").ap()

    mats = [("g", GQ, GS, GZ, WG), ("u", UQ, US, UZ, WU)]
    units = [(mi, g) for mi in range(2) for g in range(32)]
    NU = len(units)
    ND = 86
    C, N = 172, 1376

    with ExitStack() as st:
        sb = lambda *a: st.enter_context(nc.sbuf_tensor(*a))
        ps = lambda *a: st.enter_context(nc.psum_tensor(*a))
        qw_sb = [sb("gqs", [128, 32, 172], I32), sb("uqs", [128, 32, 172], I32),
                 sb("dqs", [128, 86, 64], I32)]
        ones = sb("ones", [1, 128], F32)
        zqr = [sb(f"zqr{i}", [1, 172], I32) for i in range(2)]
        srow = [sb(f"srow{i}", [1, 1376], F32) for i in range(2)]
        znib = sb("znib", [1, 1376], I32)
        zb = sb("zb", [1, 1376], F32)
        nibi = sb("nibi", [128, 1376], I32)
        nibb = sb("nibb", [128, 1376], BF16)
        tb = sb("tb", [128, 1376], F32)
        wout = [sb(f"wout{i}", [128, 1376], BF16) for i in range(2)]
        dzt = [sb(f"dzt{i}", [128, 64], I32) for i in range(2)]
        dst = [sb(f"dst{i}", [128, 512], F32) for i in range(2)]
        znibt = sb("znibt", [128, 512], I32)
        zft = sb("zft", [128, 512], F32)
        nibi2 = sb("nibi2", [128, 512], I32)
        nibb2 = sb("nibb2", [128, 512], BF16)
        tb2 = sb("tb2", [128, 512], F32)
        wod = [sb(f"wod{i}", [128, 512], BF16) for i in range(2)]
        psS = ps("psS", [128, 1376], F32)
        psZ = ps("psZ", [128, 1376], F32)
        ldg = st.enter_context(nc.semaphore("ldg"))
        lzr = [st.enter_context(nc.semaphore(f"lzr{i}")) for i in range(2)]
        srw = [st.enter_context(nc.semaphore(f"srw{i}")) for i in range(2)]
        dldp = [st.enter_context(nc.semaphore(f"dldp{i}")) for i in range(2)]
        sdw = [st.enter_context(nc.semaphore(f"sdw{i}")) for i in range(2)]
        zn = st.enter_context(nc.semaphore("zn"))
        zc = st.enter_context(nc.semaphore("zc"))
        bc = st.enter_context(nc.semaphore("bc"))
        wn = st.enter_context(nc.semaphore("wn"))
        vsf = st.enter_context(nc.semaphore("vsf"))
        wv = st.enter_context(nc.semaphore("wv"))
        so = st.enter_context(nc.semaphore("so"))
        dld = st.enter_context(nc.semaphore("dld"))
        dzn = st.enter_context(nc.semaphore("dzn"))
        dzc = st.enter_context(nc.semaphore("dzc"))
        dwn = st.enter_context(nc.semaphore("dwn"))
        dvs = st.enter_context(nc.semaphore("dvs"))
        dwv = st.enter_context(nc.semaphore("dwv"))
        block = st.enter_context(nc.Block())

        @block.gpsimd
        def _(g_):
            g_.dma_start(out=qw_sb[0][:], in_=GQ.rearrange("(g p) c -> p g c", p=128)).then_inc(ldg, 16)
            g_.dma_start(out=qw_sb[1][:], in_=UQ.rearrange("(g p) c -> p g c", p=128)).then_inc(ldg, 16)
            g_.dma_start(out=qw_sb[2][:], in_=DQ.rearrange("(g p) c -> p g c", p=128)).then_inc(ldg, 16)
            g_.dma_start(out=ones[:], in_=ONE).then_inc(ldg, 16)

        @block.sync
        def _(sy):
            def store(u2):
                mi2, g2 = units[u2]
                sy.wait_ge(wv, u2 + 1)
                sy.dma_start(out=mats[mi2][4][128 * g2:128 * (g2 + 1), :],
                             in_=wout[u2 % 2][:]).then_inc(srw[u2 % 2], 16)
            for u, (mi, g) in enumerate(units):
                ZZ, SS = mats[mi][3], mats[mi][2]
                if u >= 2:
                    sy.wait_ge(zn, u - 1)
                    sy.wait_ge(bc, u - 1)
                sy.dma_start(out=zqr[u % 2][:], in_=ZZ[g:g + 1, :]).then_inc(lzr[u % 2], 16)
                sy.dma_start(out=srow[u % 2][:], in_=SS[g:g + 1, :]).then_inc(lzr[u % 2], 16)
                if u >= 2:
                    store(u - 2)
            store(NU - 2)
            store(NU - 1)
            def dstore(u2):
                sy.wait_ge(dwv, u2 + 1)
                sy.dma_start(out=WD[128 * u2:128 * (u2 + 1), :],
                             in_=wod[u2 % 2][:]).then_inc(sdw[u2 % 2], 16)
            for ud in range(ND):
                if ud >= 2:
                    sy.wait_ge(dzn, ud - 1)
                    sy.wait_ge(dwv, ud - 1)
                sy.dma_start(out=dzt[ud % 2][:], in_=DZE[128 * ud:128 * (ud + 1), :]).then_inc(dldp[ud % 2], 16)
                sy.dma_start(out=dst[ud % 2][:], in_=DSE[128 * ud:128 * (ud + 1), :]).then_inc(dldp[ud % 2], 16)
                if ud >= 2:
                    dstore(ud - 2)
            dstore(ND - 2)
            dstore(ND - 1)
            sy.wait_ge(srw[0], 16 * (NU // 2))
            sy.wait_ge(srw[1], 16 * (NU // 2))
            sy.wait_ge(sdw[0], 16 * 43)
            sy.wait_ge(sdw[1], 16 * 43)

        @block.vector
        def _(v):
            v.wait_ge(ldg, 64)
            for u, (mi, g) in enumerate(units):
                v.wait_ge(lzr[u % 2], 32 * (u // 2 + 1))
                if u >= 1:
                    v.wait_ge(zc, 2 * (u - 1) + 1)
                for p in range(8):
                    inst = v.tensor_scalar(znib[:, p * C:(p + 1) * C], zqr[u % 2][:],
                                           4 * ARO[p], 15, ALU.logical_shift_right, ALU.bitwise_and)
                inst.then_inc(zn, 1)
                if u >= 1:
                    v.wait_ge(zc, 2 * (u - 1) + 2)
                for p in range(8):
                    inst = v.tensor_scalar(nibi[:, p * C:(p + 1) * C], qw_sb[mi][:, g, :],
                                           4 * ARO[p], 15, ALU.logical_shift_right, ALU.bitwise_and)
                inst.then_inc(wn, 1)
                v.wait_ge(bc, u + 1)
                v.wait_ge(zc, 2 * u + 2)
                if u >= 1:
                    v.wait_ge(wv, u)
                v.tensor_tensor(tb[:], nibb[:], psZ[:], ALU.subtract).then_inc(vsf, 1)
                v.wait_ge(vsf, u + 1)
                if u >= 2:
                    v.wait_ge(srw[u % 2], 16 * (u // 2))
                v.tensor_tensor(wout[u % 2][:], tb[:], psS[:], ALU.mult).then_inc(wv, 1)
            for ud in range(ND):
                v.wait_ge(dldp[ud % 2], 32 * (ud // 2 + 1))
                if ud >= 1:
                    v.wait_ge(dzc, 2 * (ud - 1) + 1)
                for p in range(8):
                    inst = v.tensor_scalar(znibt[:, p * 64:(p + 1) * 64], dzt[ud % 2][:],
                                           4 * ARO[p], 15, ALU.logical_shift_right, ALU.bitwise_and)
                inst.then_inc(dzn, 1)
                if ud >= 1:
                    v.wait_ge(dzc, 2 * (ud - 1) + 2)
                for p in range(8):
                    inst = v.tensor_scalar(nibi2[:, p * 64:(p + 1) * 64], qw_sb[2][:, ud, :],
                                           4 * ARO[p], 15, ALU.logical_shift_right, ALU.bitwise_and)
                inst.then_inc(dwn, 1)
                v.wait_ge(dzc, 2 * ud + 2)
                if ud >= 1:
                    v.wait_ge(dwv, ud)
                v.tensor_tensor(tb2[:], nibb2[:], zft[:], ALU.subtract).then_inc(dvs, 1)
                v.wait_ge(dvs, ud + 1)
                if ud >= 2:
                    v.wait_ge(sdw[ud % 2], 16 * (ud // 2))
                v.tensor_tensor(wod[ud % 2][:], tb2[:], dst[ud % 2][:], ALU.mult).then_inc(dwv, 1)

        @block.scalar
        def _(s):
            for u, (mi, g) in enumerate(units):
                s.wait_ge(zn, u + 1)
                if u >= 1:
                    s.wait_ge(bc, u)
                s.activation(zb[:], znib[:], AF.Copy).then_inc(zc, 1)
                s.wait_ge(wn, u + 1)
                if u >= 1:
                    s.wait_ge(vsf, u)
                s.activation(nibb[:], nibi[:], AF.Copy).then_inc(zc, 1)
            for ud in range(ND):
                s.wait_ge(dzn, ud + 1)
                if ud >= 1:
                    s.wait_ge(dvs, ud)
                s.activation(zft[:], znibt[:], AF.Copy).then_inc(dzc, 1)
                s.wait_ge(dwn, ud + 1)
                if ud >= 1:
                    s.wait_ge(dvs, ud)
                s.activation(nibb2[:], nibi2[:], AF.Copy).then_inc(dzc, 1)

        @block.tensor
        def _(t):
            t.wait_ge(ldg, 64)
            for u, (mi, g) in enumerate(units):
                t.wait_ge(zc, 2 * u + 1)
                t.wait_ge(lzr[u % 2], 32 * (u // 2 + 1))
                if u >= 1:
                    t.wait_ge(wv, u)
                for ci, cw in [(0, 512), (1, 512), (2, 352)]:
                    t.matmul(psS[:, 512 * ci:512 * ci + cw], ones[:],
                             srow[u % 2][:, 512 * ci:512 * ci + cw], start=True, stop=True)
                for ci, cw in [(0, 512), (1, 512), (2, 352)]:
                    inst = t.matmul(psZ[:, 512 * ci:512 * ci + cw], ones[:],
                                    zb[:, 512 * ci:512 * ci + cw], start=True, stop=True)
                inst.then_inc(bc, 1)
    return nc


def host_dequant_inputs(gq, gs, gz, uq, us, uz, dq, ds, dz):
    lp_gate = lperm_global(172)
    lp_down = lperm_global(64)
    dq_perm = dq[lp_gate, :]
    ds_exp = np.repeat(ds, 128, axis=0)[lp_gate, :]
    dz_exp = np.repeat(dz, 128, axis=0)[lp_gate, :]
    one = np.ones((1, 128), np.float32)
    maps = []
    for c in range(8):
        maps.append(dict(
            gq=cc(gq[:, 172 * c:172 * (c + 1)]),
            gs=cc(gs[:, lp_gate[1376 * c:1376 * (c + 1)]]),
            gz=cc(gz[:, 172 * c:172 * (c + 1)]),
            uq=cc(uq[:, 172 * c:172 * (c + 1)]),
            us=cc(us[:, lp_gate[1376 * c:1376 * (c + 1)]]),
            uz=cc(uz[:, 172 * c:172 * (c + 1)]),
            dq=cc(dq_perm[:, 64 * c:64 * (c + 1)]),
            dse=cc(ds_exp[:, lp_down[512 * c:512 * (c + 1)]]),
            dze=cc(dz_exp[:, 64 * c:64 * (c + 1)]),
            one=one,
        ))
    return maps, lp_gate, lp_down


def build_ffn0():
    nc = bass.Bass("TRN2", target_bir_lowering=False)
    XNT = nc.dram_tensor("xnT", [D, T], BF16, kind="ExternalInput").ap()
    WGS = nc.dram_tensor("wgs", [86, D, 128], BF16, kind="ExternalInput").ap()
    WUS = nc.dram_tensor("wus", [86, D, 128], BF16, kind="ExternalInput").ap()
    HT = nc.dram_tensor("hT", [11008, T], BF16, kind="ExternalOutput").ap()

    NG = 86 * 2
    with ExitStack() as st:
        sb = lambda *a: st.enter_context(nc.sbuf_tensor(*a))
        ps = lambda *a: st.enter_context(nc.psum_tensor(*a))
        xnT = sb("xnTs", [128, 32, T], BF16)
        wg = [sb(f"wg{i}", [128, 32, 128], BF16) for i in range(2)]
        wu = [sb(f"wu{i}", [128, 32, 128], BF16) for i in range(2)]
        sg = [sb(f"sg{i}", [128, 512], F32) for i in range(2)]
        ht = [sb(f"ht{i}", [128, 512], BF16) for i in range(2)]
        psG = [ps(f"psG{i}", [128, 512], F32) for i in range(2)]
        psU = [ps(f"psU{i}", [128, 512], F32) for i in range(2)]
        lds = st.enter_context(nc.semaphore("lds"))
        lgw = [st.enter_context(nc.semaphore(f"lgw{i}")) for i in range(2)]
        luw = [st.enter_context(nc.semaphore(f"luw{i}")) for i in range(2)]
        sht = [st.enter_context(nc.semaphore(f"sht{i}")) for i in range(2)]
        mmg = st.enter_context(nc.semaphore("mmg"))
        sgs = st.enter_context(nc.semaphore("sgs"))
        hv = st.enter_context(nc.semaphore("hv"))
        so = st.enter_context(nc.semaphore("so"))
        block = st.enter_context(nc.Block())

        @block.sync
        def _(sy):
            sy.dma_start(out=xnT[:], in_=XNT.rearrange("(j p) t -> p j t", p=128)).then_inc(lds, 16)
            for gi in range(NG):
                nt, th = gi // 2, gi % 2
                sy.wait_ge(hv, gi + 1)
                sy.dma_start(out=HT[128 * nt:128 * (nt + 1), 512 * th:512 * (th + 1)],
                             in_=ht[gi % 2][:]).then_inc(sht[gi % 2], 16)
            sy.wait_ge(sht[0], 16 * (NG // 2))
            sy.wait_ge(sht[1], 16 * (NG // 2))

        @block.gpsimd
        def _(g):
            for nt in range(86):
                if nt >= 2:
                    g.wait_ge(mmg, 4 * (nt - 1))
                g.dma_start(out=wg[nt % 2][:], in_=WGS[nt].rearrange("(j p) n -> p j n", p=128)).then_inc(lgw[nt % 2], 16)
                g.dma_start(out=wu[nt % 2][:], in_=WUS[nt].rearrange("(j p) n -> p j n", p=128)).then_inc(luw[nt % 2], 16)

        @block.tensor
        def _(t):
            t.wait_ge(lds, 16)
            for gi in range(NG):
                nt, th = gi // 2, gi % 2
                if th == 0:
                    t.wait_ge(lgw[nt % 2], 16 * (nt // 2 + 1))
                    t.wait_ge(luw[nt % 2], 16 * (nt // 2 + 1))
                if gi >= 2:
                    t.wait_ge(sgs, gi - 1)
                    t.wait_ge(hv, gi - 1)
                sl = slice(512 * th, 512 * (th + 1))
                for j in range(32):
                    inst = t.matmul(psG[gi % 2][:], wg[nt % 2][:, j, :], xnT[:, j, sl],
                                    start=(j == 0), stop=(j == 31))
                inst.then_inc(mmg, 1)
                for j in range(32):
                    inst = t.matmul(psU[gi % 2][:], wu[nt % 2][:, j, :], xnT[:, j, sl],
                                    start=(j == 0), stop=(j == 31))
                inst.then_inc(mmg, 1)

        @block.scalar
        def _(s):
            for gi in range(NG):
                s.wait_ge(mmg, 2 * gi + 1)
                if gi >= 2:
                    s.wait_ge(hv, gi - 1)
                s.activation(sg[gi % 2][:], psG[gi % 2][:], AF.Silu).then_inc(sgs, 1)

        @block.vector
        def _(v):
            for gi in range(NG):
                v.wait_ge(mmg, 2 * gi + 2)
                v.wait_ge(sgs, gi + 1)
                if gi >= 2:
                    v.wait_ge(sht[gi % 2], 16 * (gi // 2))
                v.tensor_tensor(ht[gi % 2][:], sg[gi % 2][:], psU[gi % 2][:], ALU.mult).then_inc(hv, 1)
    return nc


def build_downwo():
    nc = bass.Bass("TRN2", target_bir_lowering=False)
    HT = nc.dram_tensor("hT", [11008, T], BF16, kind="ExternalInput").ap()
    WDS = nc.dram_tensor("wds", [32, 11008, 128], BF16, kind="ExternalInput").ap()
    CXT = nc.dram_tensor("cxT", [D, T], BF16, kind="ExternalInput").ap()
    WOS = nc.dram_tensor("wos", [32, D, 128], F32, kind="ExternalInput").ap()
    XTP = nc.dram_tensor("xTp", [D, T], F32, kind="ExternalInput").ap()
    ONE = nc.dram_tensor("one", [1, 128], F32, kind="ExternalInput").ap()
    ONEC = nc.dram_tensor("onec", [128, 1], F32, kind="ExternalInput").ap()
    X1T = nc.dram_tensor("x1T", [D, T], F32, kind="ExternalOutput").ap()
    XN1T = nc.dram_tensor("xn1T", [D, T], BF16, kind="ExternalOutput").ap()

    with ExitStack() as st:
        sb = lambda *a: st.enter_context(nc.sbuf_tensor(*a))
        ps = lambda *a: st.enter_context(nc.psum_tensor(*a))
        hth = sb("hth", [128, 86, 256], BF16)
        cxh = sb("cxh", [128, 32, 256], BF16)
        g0h = sb("g0h", [128, 32, 256], BF16)
        x1bh = sb("x1bh", [128, 32, 256], BF16)
        wd = [sb(f"wd{i}", [128, 86, 128], BF16) for i in range(2)]
        wo = [sb(f"wo{i}", [128, 32, 128], BF16) for i in range(2)]
        xt = [sb(f"xt{i}", [128, 256], F32) for i in range(2)]
        s0 = [sb(f"s0{i}", [128, 256], F32) for i in range(2)]
        tsb = sb("tsb", [128, 256], F32)
        x1s = [sb(f"x1s{i}", [128, 256], F32) for i in range(2)]
        x1q = [sb(f"x1q{i}", [128, 256], BF16) for i in range(2)]
        xn1 = [sb(f"xn1{i}", [128, 256], BF16) for i in range(2)]
        onesc = sb("onesc", [128, 1], BF16)
        oner = sb("oner", [1, 128], F32)
        mrow = sb("mrow", [1, 256], F32)
        t1row = sb("t1row", [1, 256], F32)
        vrow = sb("vrow", [1, 256], F32)
        hrow = sb("hrow", [1, 256], F32)
        rstdrow = sb("rstdrow", [1, 256], F32)
        nbrow = sb("nbrow", [1, 256], F32)
        psD = ps("psD", [128, 256], F32)
        psS = ps("psS2", [128, 256], F32)
        psSum = ps("psSum", [1, 256], F32)
        psSq = ps("psSq", [1, 256], F32)
        psR = ps("psR", [128, 256], F32)
        psN = ps("psN", [128, 256], F32)
        lo = st.enter_context(nc.semaphore("lo"))
        lh = st.enter_context(nc.semaphore("lh"))
        lc = st.enter_context(nc.semaphore("lc"))
        lon = st.enter_context(nc.semaphore("lon"))
        lwd = [st.enter_context(nc.semaphore(f"lwd{i}")) for i in range(2)]
        lwo = [st.enter_context(nc.semaphore(f"lwo{i}")) for i in range(2)]
        lxp = [st.enter_context(nc.semaphore(f"lxp{i}")) for i in range(2)]
        sx1 = [st.enter_context(nc.semaphore(f"sx1{i}")) for i in range(2)]
        sxn1 = [st.enter_context(nc.semaphore(f"sxn1{i}")) for i in range(2)]
        mmd = st.enter_context(nc.semaphore("mmd"))
        mms = st.enter_context(nc.semaphore("mms"))
        mmq = st.enter_context(nc.semaphore("mmq"))
        gsl = st.enter_context(nc.semaphore("gsl"))
        sc0 = st.enter_context(nc.semaphore("sc0"))
        sxc = st.enter_context(nc.semaphore("sxc"))
        vv = st.enter_context(nc.semaphore("vv"))
        vw = st.enter_context(nc.semaphore("vw"))
        sln = st.enter_context(nc.semaphore("sln"))
        bcm = st.enter_context(nc.semaphore("bcm"))
        vn = st.enter_context(nc.semaphore("vn"))
        so = st.enter_context(nc.semaphore("so"))
        block = st.enter_context(nc.Block())

        def x1_store_count(u):
            hf2, ot2 = divmod(u, 32)
            return 64 * hf2 + ot2 + 1

        def xn1_store_count(u):
            hf2, ot2 = divmod(u, 32)
            return 64 * hf2 + 32 + ot2 + 1

        @block.sync
        def _(sy):
            sy.dma_start(out=oner[:], in_=ONE).then_inc(lo, 16)
            for hf in range(4):
                if hf >= 1:
                    sy.wait_ge(mmd, 32 * hf)
                sy.dma_start(out=hth[:], in_=HT.rearrange("(j p) t -> p j t", p=128)[:, :, 256 * hf:256 * (hf + 1)]).then_inc(lh, 16)
                if hf >= 1:
                    sy.wait_ge(mms, 32 * hf)
                sy.dma_start(out=cxh[:], in_=CXT.rearrange("(j p) t -> p j t", p=128)[:, :, 256 * hf:256 * (hf + 1)]).then_inc(lc, 16)
                for ot in range(32):
                    u = 32 * hf + ot
                    if u >= 2:
                        sy.wait_ge(vv, 2 * (u - 2) + 2)
                    sy.dma_start(out=xt[ot % 2][:], in_=XTP[128 * ot:128 * (ot + 1), 256 * hf:256 * (hf + 1)]).then_inc(lxp[u % 2], 16)
                    if ot >= 2:
                        sy.dma_start(out=X1T[128 * (ot - 2):128 * (ot - 1), 256 * hf:256 * (hf + 1)],
                                     in_=x1s[ot % 2][:]).then_inc(sx1[ot % 2], 16)
                for ot in [30, 31]:
                    sy.wait_ge(vv, 2 * (32 * hf + ot) + 2)
                    sy.dma_start(out=X1T[128 * ot:128 * (ot + 1), 256 * hf:256 * (hf + 1)],
                                 in_=x1s[ot % 2][:]).then_inc(sx1[ot % 2], 16)
                for ot in range(32):
                    sy.wait_ge(vn, 2 * (32 * hf + ot) + 2)
                    sy.dma_start(out=XN1T[128 * ot:128 * (ot + 1), 256 * hf:256 * (hf + 1)],
                                 in_=xn1[ot % 2][:]).then_inc(sxn1[ot % 2], 16)
            for par in range(2):
                sy.wait_ge(sx1[par], 16 * 64)
                sy.wait_ge(sxn1[par], 16 * 64)

        @block.gpsimd
        def _(g):
            g.dma_start(out=onesc[:], in_=ONEC).then_inc(lon, 16)
            for hf in range(4):
                for ot in range(32):
                    u = 32 * hf + ot
                    if u >= 2:
                        g.wait_ge(mmd, u - 1)
                    g.dma_start(out=wd[ot % 2][:], in_=WDS[ot].rearrange("(j p) n -> p j n", p=128)).then_inc(lwd[ot % 2], 16)
                for ot in range(32):
                    u = 32 * hf + ot
                    if u >= 2:
                        g.wait_ge(mms, u - 1)
                    g.dma_start(out=wo[ot % 2][:], in_=WOS[ot].rearrange("(j p) n -> p j n", p=128)).then_inc(lwo[ot % 2], 16)

        @block.tensor
        def _(t):
            for hf in range(4):
                t.wait_ge(lh, 16 * (hf + 1))
                for ot in range(32):
                    u = 32 * hf + ot
                    t.wait_ge(lwd[ot % 2], 16 * (u // 2 + 1))
                    if u >= 1:
                        t.wait_ge(gsl, u)
                    for j in range(86):
                        inst = t.matmul(psD[:], wd[ot % 2][:, j, :], hth[:, j, :],
                                        start=(j == 0), stop=(j == 85))
                    inst.then_inc(mmd, 1)
                t.wait_ge(lc, 16 * (hf + 1))
                if hf >= 1:
                    t.wait_ge(vw, 6 * hf)
                    t.wait_ge(vn, 64 * hf)
                for ot in range(34):
                    if ot < 32:
                        u = 32 * hf + ot
                        t.wait_ge(lwo[ot % 2], 16 * (u // 2 + 1))
                        if u >= 1:
                            t.wait_ge(sc0, u)
                        for j in range(32):
                            inst = t.matmul(psS[:], wo[ot % 2][:, j, :], cxh[:, j, :],
                                            start=(j == 0), stop=(j == 31))
                        inst.then_inc(mms, 1)
                    if ot >= 2:
                        o2 = ot - 2
                        u2 = 32 * hf + o2
                        t.wait_ge(sxc, 2 * u2 + 2)
                        if hf == 0 and o2 == 0:
                            t.wait_ge(lon, 16)
                        t.matmul(psSum[:], onesc[:], x1bh[:, o2, :],
                                 start=(o2 == 0), stop=(o2 == 31))
                        inst = t.matmul(psSq[:], onesc[:], x1q[o2 % 2][:],
                                        start=(o2 == 0), stop=(o2 == 31))
                        inst.then_inc(mmq, 1)
                t.wait_ge(sln, 2 * hf + 2)
                t.wait_ge(vw, 6 * hf + 6)
                if hf == 0:
                    t.wait_ge(lo, 16)
                t.matmul(psR[:], oner[:], rstdrow[:], start=True, stop=True)
                inst = t.matmul(psN[:], oner[:], nbrow[:], start=True, stop=True)
                inst.then_inc(bcm, 1)

        @block.scalar
        def _(s):
            for hf in range(4):
                for ot in range(32):
                    u = 32 * hf + ot
                    s.wait_ge(mmd, u + 1)
                    if hf >= 1 and ot == 0:
                        s.wait_ge(vv, 64 * hf)
                    s.activation(g0h[:, ot, :], psD[:], AF.Silu).then_inc(gsl, 1)
                for ot in range(32):
                    u = 32 * hf + ot
                    s.wait_ge(mms, u + 1)
                    if u >= 2:
                        s.wait_ge(vv, 2 * (u - 2) + 1)
                    s.activation(s0[ot % 2][:], psS[:], AF.Copy).then_inc(sc0, 1)
                    s.wait_ge(vv, 2 * u + 2)
                    if hf >= 1 and ot == 0:
                        s.wait_ge(vn, 64 * hf)
                    if u >= 2:
                        s.wait_ge(mmq, u - 1)
                    s.activation(x1bh[:, ot, :], x1s[ot % 2][:], AF.Copy)
                    inst = s.activation(x1q[ot % 2][:], x1s[ot % 2][:], AF.Square)
                    inst.then_inc(sxc, 2)
                s.wait_ge(vw, 6 * hf + 4)
                s.activation(vrow[:], hrow[:], AF.Ln).then_inc(sln, 1)
                s.wait_ge(vw, 6 * hf + 5)
                s.activation(rstdrow[:], hrow[:], AF.Exp).then_inc(sln, 1)

        @block.vector
        def _(v):
            for hf in range(4):
                for ot in range(32):
                    u = 32 * hf + ot
                    v.wait_ge(gsl, u + 1)
                    v.wait_ge(sc0, u + 1)
                    v.wait_ge(lxp[u % 2], 16 * (u // 2 + 1))
                    if hf >= 1 and ot == 0:
                        v.wait_ge(vn, 64 * hf)
                    if u >= 1:
                        v.wait_ge(vv, 2 * u)
                    v.tensor_tensor(tsb[:], g0h[:, ot, :], s0[ot % 2][:], ALU.mult).then_inc(vv, 1)
                    v.wait_ge(vv, 2 * u + 1)
                    if u >= 2:
                        v.wait_ge(sx1[u % 2], 16 * ((u - 2) // 2 + 1) if False else 16 * (u // 2))
                    v.tensor_tensor(x1s[ot % 2][:], tsb[:], xt[ot % 2][:], ALU.add).then_inc(vv, 1)
                v.wait_ge(mmq, 32 * hf + 32)
                vb = 6 * hf
                v.tensor_scalar(mrow[:], psSum[:], 1.0 / D, None, ALU.mult).then_inc(vw, 1)
                v.wait_ge(vw, vb + 1)
                v.tensor_tensor(t1row[:], mrow[:], mrow[:], ALU.mult).then_inc(vw, 1)
                v.wait_ge(vw, vb + 2)
                v.scalar_tensor_tensor(vrow[:], psSq[:], 1.0 / D, t1row[:], ALU.mult, ALU.subtract).then_inc(vw, 1)
                v.wait_ge(vw, vb + 3)
                v.tensor_scalar(hrow[:], vrow[:], EPS, None, ALU.add).then_inc(vw, 1)
                v.wait_ge(sln, 2 * hf + 1)
                v.tensor_scalar(hrow[:], vrow[:], -0.5, None, ALU.mult).then_inc(vw, 1)
                v.wait_ge(sln, 2 * hf + 2)
                v.scalar_tensor_tensor(nbrow[:], mrow[:], -1.0, rstdrow[:], ALU.mult, ALU.mult).then_inc(vw, 1)
                v.wait_ge(vv, 64 * hf + 64)
                v.wait_ge(bcm, hf + 1)
                v.wait_ge(sxc, 2 * (32 * hf + 31) + 2)
                for ot in range(32):
                    u = 32 * hf + ot
                    if ot >= 1:
                        v.wait_ge(vn, 2 * u)
                    v.tensor_tensor(tsb[:], x1bh[:, ot, :], psR[:], ALU.mult).then_inc(vn, 1)
                    v.wait_ge(vn, 2 * u + 1)
                    if u >= 2:
                        v.wait_ge(sxn1[u % 2], 16 * (u // 2))
                    v.tensor_tensor(xn1[ot % 2][:], tsb[:], psN[:], ALU.add).then_inc(vn, 1)
    return nc


def build_attn1(stage=2, nkv=NKV):
    nc = bass.Bass("TRN2", target_bir_lowering=False)
    XNT = nc.dram_tensor("xn1T", [D, T], BF16, kind="ExternalInput").ap()
    WQS = nc.dram_tensor("wqs", [32, D, 128], F32, kind="ExternalInput").ap()
    C1 = nc.dram_tensor("c1", [HD, T], F32, kind="ExternalInput").ap()
    S1 = nc.dram_tensor("s1", [HD, T], F32, kind="ExternalInput").ap()
    KC = nc.dram_tensor("kc", [512, NKV, 16, HD], F32, kind="ExternalInput").ap()
    VC = nc.dram_tensor("vc", [512, NKV, 16, HD], F32, kind="ExternalInput").ap()
    BTI = nc.dram_tensor("bti", [128, 2], I32, kind="ExternalInput").ap()
    EYE = nc.dram_tensor("eye", [128, 128], F32, kind="ExternalInput").ap()
    CTX1 = nc.dram_tensor("ctx1", [NKV, 128, 32, 128], BF16, kind="ExternalOutput").ap()

    with ExitStack() as st:
        sb = lambda *a: st.enter_context(nc.sbuf_tensor(*a))
        ps = lambda *a: st.enter_context(nc.psum_tensor(*a))
        xnT = sb("xnTs", [128, 32, T], BF16)
        q1T = sb("q1Ts", [128, 32, T], BF16)
        wq = [sb(f"wq{i}", [128, 32, 128], BF16) for i in range(2)]
        c1s = sb("c1s", [128, T], F32)
        s1s = sb("s1s", [128, T], F32)
        rt1 = sb("rt1", [128, 512], F32)
        rt2 = sb("rt2", [128, 512], F32)
        eyeb = sb("eyeb", [128, 128], BF16)
        idx = sb("idx", [128, 2], I32)
        kgB = sb("kgB", [128, 2, 16, 128], BF16)
        vstage = sb("vstage", [128, 2048], BF16)
        vgB = sb("vgB", [128, 2, 16, 129], BF16)
        kT1 = sb("kT1", [128, 32, 128], BF16)
        expb = [sb(f"expb{i}", [128, 512], BF16) for i in range(2)]
        rcp = sb("rcp", [128, 4], F32)
        cxg = [sb(f"cxg{i}", [128, 32, 128], BF16) for i in range(2)]
        pmm = [ps(f"pmm{i}", [128, 512], F32) for i in range(2)]
        ptp = ps("ptp", [128, 128], BF16)
        ctp = [ps(f"ctp{i}", [128, 129], F32) for i in range(4)]
        lds = st.enter_context(nc.semaphore("lds"))
        lxn = st.enter_context(nc.semaphore("lxn"))
        ley = st.enter_context(nc.semaphore("ley"))
        lwq = [st.enter_context(nc.semaphore(f"lwq{i}")) for i in range(2)]
        kf = [st.enter_context(nc.semaphore(f"kf{i}")) for i in range(2)]
        vfp = [st.enter_context(nc.semaphore(f"vfp{i}")) for i in range(2)]
        sf = [st.enter_context(nc.semaphore(f"sf{i}")) for i in range(2)]
        msd = st.enter_context(nc.semaphore("msd"))
        mmp = st.enter_context(nc.semaphore("mmp"))
        evp = st.enter_context(nc.semaphore("evp"))
        vvp = st.enter_context(nc.semaphore("vvp"))
        vcp = st.enter_context(nc.semaphore("vcp"))
        tpf = st.enter_context(nc.semaphore("tpf"))
        tcf = st.enter_context(nc.semaphore("tcf"))
        mmf = st.enter_context(nc.semaphore("mmf"))
        exf = st.enter_context(nc.semaphore("exf"))
        avf = st.enter_context(nc.semaphore("avf"))
        vrc = st.enter_context(nc.semaphore("vrc"))
        cte = st.enter_context(nc.semaphore("cte"))
        sof = st.enter_context(nc.semaphore("sof"))
        block = st.enter_context(nc.Block())

        def av_tile(t, n):
            g, ti = divmod(n, 256)
            qh, th, kt = ti // 64, (ti // 32) % 2, ti % 32
            half, j = kt // 16, kt % 16
            t.wait_ge(exf, n + 1)
            t.wait_ge(vcp, 2 * g + half + 1)
            if kt == 0:
                t.wait_ge(msd, 1)
                gidx = n // 32
                if gidx >= 1:
                    t.wait_ge(cte, gidx)
            for qs4 in range(4):
                inst = t.matmul(ctp[qs4][:], expb[n % 2][:, 128 * qs4:128 * (qs4 + 1)],
                                vgB[:, half, j, :], start=(kt == 0), stop=(kt == 31))
            inst.then_inc(avf, 1)

        @block.sync
        def _(sy):
            sy.dma_start(out=xnT[:], in_=XNT.rearrange("(j p) t -> p j t", p=128)).then_inc(lxn, 16)
            sy.dma_start(out=c1s[:], in_=C1).then_inc(lds, 16)
            sy.dma_start(out=s1s[:], in_=S1).then_inc(lds, 16)
            sy.dma_start(out=idx[:], in_=BTI).then_inc(lds, 16)
            if stage < 2:
                sy.wait_ge(evp, 64)
                sy.dma_start(out=CTX1.rearrange("g p a b -> p (g a b)", p=128) if False else CTX1.rearrange("g p a b -> p g (a b)"),
                             in_=q1T[:].rearrange("p a t -> p (a t)").rearrange("p (g x) -> p g x", g=8)).then_inc(sof, 16)
                sy.wait_ge(sof, 16)
                return
            for g in range(nkv):
                sy.wait_ge(cte, 8 * (g + 1))
                sy.dma_start(out=CTX1[g].rearrange("p a b -> p (a b)"),
                             in_=cxg[g % 2][:].rearrange("p a b -> p (a b)")).then_inc(sf[g % 2], 16)
            for par in range(2):
                sy.wait_ge(sf[par], 16 * len([g for g in range(nkv) if g % 2 == par]))

        @block.gpsimd
        def _(g_):
            g_.dma_start(out=eyeb[:], in_=EYE).then_inc(ley, 16)
            g_.memset(vgB[:, :, :, 128:129], 1.0).then_inc(msd, 1)
            for pi in range(32):
                if pi >= 2:
                    g_.wait_ge(mmp, 2 * pi - 2)
                g_.dma_start(out=wq[pi % 2][:], in_=WQS[pi].rearrange("(j p) n -> p j n", p=128)).then_inc(lwq[pi % 2], 16)
            if stage < 2:
                return
            g_.wait_ge(lds, 48)
            kcf = KC.rearrange("b h j d -> b (h j d)")
            vcf = VC.rearrange("b h j d -> b (h j d)")
            for g in range(nkv):
                if g >= 1:
                    g_.wait_ge(tpf, 32 * g)
                for half in range(2):
                    g_.indirect_dma_start(
                        out=kgB[:, half, :, :].rearrange("p a b -> p (a b)"), out_offset=None,
                        in_=kcf, in_offset=IndirectOffsetOnAxis(idx[:, half:half + 1], axis=0),
                        element_offset=g * 2048).then_inc(kf[half], 16)
                    if g >= 1 or half >= 1:
                        g_.wait_ge(vcp, 2 * g + half)
                    g_.indirect_dma_start(
                        out=vstage[:], out_offset=None,
                        in_=vcf, in_offset=IndirectOffsetOnAxis(idx[:, half:half + 1], axis=0),
                        element_offset=g * 2048).then_inc(vfp[half], 16)

        @block.tensor
        def _(t):
            t.wait_ge(lxn, 16)
            for G in range(64):
                pi, th = G // 2, G % 2
                if th == 0:
                    t.wait_ge(lwq[pi % 2], 16 * (pi // 2 + 1))
                if G >= 2:
                    t.wait_ge(evp, G - 1)
                for j in range(32):
                    inst = t.matmul(pmm[G % 2][:], wq[pi % 2][:, j, :],
                                    xnT[:, j, 512 * th:512 * (th + 1)],
                                    start=(j == 0), stop=(j == 31))
                inst.then_inc(mmp, 1)
            if stage < 2:
                return
            pending = None
            for g in range(nkv):
                if pending is not None:
                    av_tile(t, pending)
                    pending = None
                if g == 0:
                    t.wait_ge(ley, 16)
                for kt in range(32):
                    n_t = 32 * g + kt
                    t.wait_ge(kf[kt // 16], 16 * (g + 1))
                    if n_t >= 1:
                        t.wait_ge(tcf, n_t)
                    t.transpose(ptp[:], kgB[:, kt // 16, kt % 16, :], eyeb[:]).then_inc(tpf, 1)
                for ti in range(256):
                    n = 256 * g + ti
                    qh, th, kt = ti // 64, (ti // 32) % 2, ti % 32
                    h = 4 * g + qh
                    if ti == 0:
                        t.wait_ge(tcf, 32 * (g + 1))
                        t.wait_ge(evp, 64)
                    if n >= 2:
                        t.wait_ge(exf, n - 1)
                    t.matmul(pmm[n % 2][:], kT1[:, kt, :], q1T[:, h, 512 * th:512 * (th + 1)],
                             start=True, stop=True).then_inc(mmf, 1)
                    if ti >= 1:
                        av_tile(t, n - 1)
                pending = 256 * (g + 1) - 1
            av_tile(t, pending)

        @block.scalar
        def _(s):
            if stage < 2:
                return
            for n in range(256 * nkv):
                if n >= 2:
                    s.wait_ge(avf, n - 1)
                s.wait_ge(mmf, n + 1)
                s.activation(expb[n % 2][:], pmm[n % 2][:], AF.Exp).then_inc(exf, 1)
                ti = n % 256
                if ti % 32 == 31:
                    gidx = n // 32
                    g, qh, th = n // 256, ti // 64, (ti // 32) % 2
                    s.wait_ge(vrc, 4 * (gidx + 1))
                    if qh == 0 and th == 0 and g >= 2:
                        s.wait_ge(sf[g % 2], 16 * (g // 2))
                    for qs4 in range(4):
                        inst = s.activation(cxg[g % 2][:, qh * 8 + th * 4 + qs4, :],
                                            ctp[qs4][:, 0:128], AF.Copy,
                                            scale=rcp[:, qs4:qs4 + 1])
                    inst.then_inc(cte, 1)

        @block.vector
        def _(v):
            v.wait_ge(lds, 48)
            for G in range(64):
                pi, th = G // 2, G % 2
                sl = slice(512 * th, 512 * (th + 1))
                p = pmm[G % 2]
                v.wait_ge(mmp, G + 1)
                if G >= 1:
                    v.wait_ge(evp, G)
                vb = 3 * G
                v.tensor_tensor(rt1[:], p[:], c1s[:, sl], ALU.mult).then_inc(vvp, 1)
                v.tensor_tensor(rt2[0:64, :], p[64:128, :], s1s[0:64, sl], ALU.mult).then_inc(vvp, 1)
                v.tensor_tensor(rt2[64:128, :], p[0:64, :], s1s[64:128, sl], ALU.mult).then_inc(vvp, 1)
                v.wait_ge(vvp, vb + 3)
                v.tensor_tensor(q1T[:, pi, sl], rt1[:], rt2[:], ALU.add).then_inc(evp, 1)
            if stage < 2:
                return
            for g in range(nkv):
                for kt in range(32):
                    n_t = 32 * g + kt
                    v.wait_ge(tpf, n_t + 1)
                    if kt == 0 and g >= 1:
                        v.wait_ge(mmf, 256 * g)
                    v.tensor_copy(kT1[:, kt, :], ptp[:]).then_inc(tcf, 1)
                for half in range(2):
                    v.wait_ge(vfp[half], 16 * (g + 1))
                    if g >= 1:
                        v.wait_ge(avf, 256 * g)
                    v.tensor_copy(vgB[:, half, :, 0:128],
                                  vstage[:].rearrange("p (a b) -> p a b", a=16)).then_inc(vcp, 1)
                for gidx in range(8 * g, 8 * (g + 1)):
                    v.wait_ge(avf, 32 * (gidx + 1))
                    if gidx >= 1:
                        v.wait_ge(cte, gidx)
                    for qs4 in range(4):
                        v.reciprocal(rcp[:, qs4:qs4 + 1], ctp[qs4][:, 128:129]).then_inc(vrc, 1)
    return nc


def build_final():
    nc = bass.Bass("TRN2", target_bir_lowering=False)
    XNT = nc.dram_tensor("xn1T", [D, T], BF16, kind="ExternalInput").ap()
    CXT = nc.dram_tensor("cx1T", [D, T], BF16, kind="ExternalInput").ap()
    X1T = nc.dram_tensor("x1T", [D, T], F32, kind="ExternalInput").ap()
    WOS = nc.dram_tensor("wos", [32, D, 128], F32, kind="ExternalInput").ap()
    WBS = nc.dram_tensor("wbs", [32, D, 128], F32, kind="ExternalInput").ap()
    BB = nc.dram_tensor("bb", [128, 32], F32, kind="ExternalInput").ap()
    WG8 = nc.dram_tensor("wg8", [D, 8], F32, kind="ExternalInput").ap()
    AT = nc.dram_tensor("at", [D, 128], F32, kind="ExternalInput").ap()
    BT8 = nc.dram_tensor("bt8", [128, D], F32, kind="ExternalInput").ap()
    E16 = nc.dram_tensor("e16", [8, 128], F32, kind="ExternalInput").ap()
    EYE = nc.dram_tensor("eye", [128, 128], F32, kind="ExternalInput").ap()
    OUT = nc.dram_tensor("outT", [D, T], F32, kind="ExternalOutput").ap()

    with ExitStack() as st:
        sb = lambda *a: st.enter_context(nc.sbuf_tensor(*a))
        ps = lambda *a: st.enter_context(nc.psum_tensor(*a))
        xnT = sb("xnTs", [128, 32, T], BF16)
        cxT = sb("cxTs", [128, 32, T], BF16)
        wo = [sb(f"wo{i}", [128, 32, 128], BF16) for i in range(2)]
        wb = [sb("wb0", [128, 32, 128], BF16)]
        x1t = [sb(f"x1t{i}", [128, 512], F32) for i in range(2)]
        bb = sb("bbs", [128, 32], F32)
        wg8 = sb("wg8s", [128, 32, 8], BF16)
        at = sb("ats", [128, 32, 128], BF16)
        bt8 = sb("bt8s", [128, D], BF16)
        e16 = sb("e16s", [8, 128], BF16)
        eyeb = sb("eyeb", [128, 128], BF16)
        eyef = sb("eyef", [8, 8], F32)
        lg = sb("lg", [8, T], F32)
        lgT = sb("lgT", [128, 8, 8], F32)
        mc = sb("mc", [128, 8], F32)
        msk1 = sb("msk1", [128, 8], F32)
        msk2 = sb("msk2", [128, 8], F32)
        l2c = sb("l2c", [128, 8], F32)
        r1 = sb("r1", [128, 8], F32)
        r2 = sb("r2", [128, 8], F32)
        routeb = sb("routeb", [128, 8], BF16)
        routeT = sb("routeT", [8, T], BF16)
        routeX = sb("routeX", [128, T], BF16)
        hw = sb("hws", [128, T], BF16)
        g1 = [sb(f"g1{i}", [128, 512], F32) for i in range(2)]
        s1 = [sb(f"s1{i}", [128, 512], F32) for i in range(2)]
        tt2 = sb("tt2", [128, 512], F32)
        ob = [sb(f"ob{i}", [128, 512], F32) for i in range(2)]
        psL = ps("psL", [8, T], F32)
        psT = ps("psT", [128, 8], F32)
        psTb = ps("psTb", [8, 128], BF16)
        psRX = ps("psRX", [128, 512], F32)
        psH = ps("psH", [128, 512], F32)
        psR2 = ps("psR2", [128, 512], F32)
        psS3 = ps("psS3", [128, 512], F32)
        lds = st.enter_context(nc.semaphore("lds"))
        lgf = st.enter_context(nc.semaphore("lgf"))
        lwb = st.enter_context(nc.semaphore("lwb"))
        lwo = [st.enter_context(nc.semaphore(f"lwo{i}")) for i in range(2)]
        lx1 = [st.enter_context(nc.semaphore(f"lx1{i}")) for i in range(2)]
        sow2 = [st.enter_context(nc.semaphore(f"sow2{i}")) for i in range(2)]
        mml = st.enter_context(nc.semaphore("mml"))
        vga = st.enter_context(nc.semaphore("vga"))
        sga = st.enter_context(nc.semaphore("sga"))
        tra = st.enter_context(nc.semaphore("tra"))
        mmx = st.enter_context(nc.semaphore("mmx"))
        mmh = st.enter_context(nc.semaphore("mmh"))
        vhw = st.enter_context(nc.semaphore("vhw"))
        mmr = st.enter_context(nc.semaphore("mmr"))
        mmo = st.enter_context(nc.semaphore("mmo"))
        sgl = st.enter_context(nc.semaphore("sgl"))
        ss1 = st.enter_context(nc.semaphore("ss1"))
        vvo = st.enter_context(nc.semaphore("vvo"))
        so = st.enter_context(nc.semaphore("so"))
        block = st.enter_context(nc.Block())

        VGA_CH = lambda tc: 9 + 9 * tc

        @block.sync
        def _(sy):
            sy.dma_start(out=xnT[:], in_=XNT.rearrange("(j p) t -> p j t", p=128)).then_inc(lds, 16)
            sy.dma_start(out=cxT[:], in_=CXT.rearrange("(j p) t -> p j t", p=128)).then_inc(lds, 16)
            sy.dma_start(out=bb[:], in_=BB).then_inc(lds, 16)
            sy.dma_start(out=eyef[:], in_=EYE[0:8, 0:8]).then_inc(lds, 16)
            for u in range(64):
                ot, th = divmod(u, 2)
                if u >= 2:
                    sy.wait_ge(vvo, 2 * (u - 2) + 2)
                sy.dma_start(out=x1t[u % 2][:], in_=X1T[128 * ot:128 * (ot + 1), 512 * th:512 * (th + 1)]).then_inc(lx1[u % 2], 16)
                if u >= 2:
                    u2 = u - 2
                    ot2, th2 = divmod(u2, 2)
                    sy.wait_ge(vvo, 2 * u2 + 2)
                    sy.dma_start(out=OUT[128 * ot2:128 * (ot2 + 1), 512 * th2:512 * (th2 + 1)],
                                 in_=ob[u2 % 2][:]).then_inc(sow2[u2 % 2], 16)
            for u2 in [62, 63]:
                ot2, th2 = divmod(u2, 2)
                sy.wait_ge(vvo, 2 * u2 + 2)
                sy.dma_start(out=OUT[128 * ot2:128 * (ot2 + 1), 512 * th2:512 * (th2 + 1)],
                             in_=ob[u2 % 2][:]).then_inc(sow2[u2 % 2], 16)
            for i in range(2):
                sy.wait_ge(sow2[i], 16 * 32)

        @block.gpsimd
        def _(g):
            g.dma_start(out=wg8[:], in_=WG8.rearrange("(j p) e -> p j e", p=128)).then_inc(lgf, 16)
            g.dma_start(out=at[:], in_=AT.rearrange("(j p) n -> p j n", p=128)).then_inc(lgf, 16)
            g.dma_start(out=bt8[:], in_=BT8).then_inc(lgf, 16)
            g.dma_start(out=e16[:], in_=E16).then_inc(lgf, 16)
            g.dma_start(out=eyeb[:], in_=EYE).then_inc(lgf, 16)
            for ot in range(32):
                if ot >= 1:
                    g.wait_ge(mmr, 2 * ot)
                g.dma_start(out=wb[0][:], in_=WBS[ot].rearrange("(j p) n -> p j n", p=128)).then_inc(lwb, 16)
                if ot >= 2:
                    g.wait_ge(mmo, 2 * ot - 2)
                g.dma_start(out=wo[ot % 2][:], in_=WOS[ot].rearrange("(j p) n -> p j n", p=128)).then_inc(lwo[ot % 2], 16)

        @block.tensor
        def _(t):
            t.wait_ge(lds, 64)
            t.wait_ge(lgf, 16 * 5)
            for th in range(2):
                for j in range(32):
                    inst = t.matmul(psL[:, 512 * th:512 * (th + 1)], wg8[:, j, :],
                                    xnT[:, j, 512 * th:512 * (th + 1)],
                                    start=(j == 0), stop=(j == 31))
                inst.then_inc(mml, 1)
            for tc in range(8):
                t.wait_ge(vga, 1)
                if tc >= 1:
                    t.wait_ge(vga, 1 + tc)
                t.transpose(psT[:, 0:8], lg[:, 128 * tc:128 * (tc + 1)], eyef[:]).then_inc(tra, 1)
            for tc in range(8):
                t.wait_ge(vga, VGA_CH(tc) + 8)
                if tc >= 1:
                    t.wait_ge(vga, VGA_CH(tc - 1) + 9)
                t.transpose(psTb[:], routeb[:], eyeb[:]).then_inc(tra, 1)
            for th in range(2):
                t.wait_ge(vga, 81 + th)
                t.matmul(psRX[:], e16[:], routeT[:, 512 * th:512 * (th + 1)],
                         start=True, stop=True).then_inc(mmx, 1)
            for th in range(2):
                t.wait_ge(vga, 83)
                if th >= 1:
                    t.wait_ge(vhw, th)
                for j in range(32):
                    inst = t.matmul(psH[:], at[:, j, :], xnT[:, j, 512 * th:512 * (th + 1)],
                                    start=(j == 0), stop=(j == 31))
                inst.then_inc(mmh, 1)
            t.wait_ge(vhw, 2)
            for ot in range(32):
                t.wait_ge(lwb, 16 * (ot + 1))
                for th in range(2):
                    u = 2 * ot + th
                    if u >= 1:
                        t.wait_ge(sgl, u)
                    for j in range(32):
                        t.matmul(psR2[:], wb[0][:, j, :], xnT[:, j, 512 * th:512 * (th + 1)],
                                 start=(j == 0), stop=False)
                    inst = t.matmul(psR2[:], bt8[:, 128 * ot:128 * (ot + 1)],
                                    hw[:, 512 * th:512 * (th + 1)], start=False, stop=True)
                    inst.then_inc(mmr, 1)
                t.wait_ge(lwo[ot % 2], 16 * (ot // 2 + 1))
                for th in range(2):
                    u = 2 * ot + th
                    if u >= 1:
                        t.wait_ge(ss1, u)
                    for j in range(32):
                        inst = t.matmul(psS3[:], wo[ot % 2][:, j, :], cxT[:, j, 512 * th:512 * (th + 1)],
                                        start=(j == 0), stop=(j == 31))
                    inst.then_inc(mmo, 1)

        @block.scalar
        def _(s):
            for tc in range(8):
                vb = VGA_CH(tc)
                s.wait_ge(vga, vb + 6)
                s.activation(mc[:, 3:4], mc[:, 2:3], AF.Sigmoid).then_inc(sga, 1)
                s.wait_ge(sga, 3 * tc + 1)
                s.activation(r1[:], msk1[:], AF.Copy, scale=mc[:, 3:4]).then_inc(sga, 1)
                s.wait_ge(vga, vb + 7)
                s.activation(r2[:], msk2[:], AF.Copy, scale=mc[:, 4:5]).then_inc(sga, 1)
            for ot in range(32):
                for th in range(2):
                    u = 2 * ot + th
                    s.wait_ge(mmr, u + 1)
                    if u >= 2:
                        s.wait_ge(vvo, 2 * (u - 2) + 1)
                    s.activation(g1[th][:], psR2[:], AF.Silu, bias=bb[:, ot:ot + 1]).then_inc(sgl, 1)
                    s.wait_ge(mmo, u + 1)
                    if u >= 2:
                        s.wait_ge(vvo, 2 * (u - 2) + 1)
                    s.activation(s1[th][:], psS3[:], AF.Copy).then_inc(ss1, 1)

        @block.vector
        def _(v):
            v.wait_ge(mml, 2)
            v.tensor_copy(lg[:], psL[:]).then_inc(vga, 1)
            for tc in range(8):
                v.wait_ge(tra, tc + 1)
                v.tensor_copy(lgT[:, tc, :], psT[:, 0:8]).then_inc(vga, 1)
            for tc in range(8):
                vb = VGA_CH(tc)
                x = lgT[:, tc, :]
                if tc >= 1:
                    v.wait_ge(sga, 3 * tc)
                v.reduce_max(mc[:, 0:1], x, axis=AX.X).then_inc(vga, 1)
                v.wait_ge(vga, vb + 1)
                v.tensor_scalar(msk1[:], x, mc[:, 0:1], None, ALU.is_ge).then_inc(vga, 1)
                v.wait_ge(vga, vb + 2)
                v.scalar_tensor_tensor(l2c[:], msk1[:], NEG, x, ALU.mult, ALU.add).then_inc(vga, 1)
                v.wait_ge(vga, vb + 3)
                v.reduce_max(mc[:, 1:2], l2c[:], axis=AX.X).then_inc(vga, 1)
                v.wait_ge(vga, vb + 4)
                v.tensor_scalar(msk2[:], l2c[:], mc[:, 1:2], None, ALU.is_ge).then_inc(vga, 1)
                v.tensor_tensor(mc[:, 2:3], mc[:, 0:1], mc[:, 1:2], ALU.subtract).then_inc(vga, 1)
                v.wait_ge(sga, 3 * tc + 1)
                v.tensor_scalar(mc[:, 4:5], mc[:, 3:4], -1.0, 1.0, ALU.mult, ALU.add).then_inc(vga, 1)
                v.wait_ge(sga, 3 * tc + 3)
                if tc >= 1:
                    v.wait_ge(tra, 8 + tc)
                v.tensor_tensor(routeb[:], r1[:], r2[:], ALU.add).then_inc(vga, 1)
                v.wait_ge(tra, 8 + tc + 1)
                v.tensor_copy(routeT[:, 128 * tc:128 * (tc + 1)], psTb[:]).then_inc(vga, 1)
            for th in range(2):
                v.wait_ge(mmx, th + 1)
                v.tensor_copy(routeX[:, 512 * th:512 * (th + 1)], psRX[:]).then_inc(vga, 1)
            for th in range(2):
                v.wait_ge(mmh, th + 1)
                v.tensor_tensor(hw[:, 512 * th:512 * (th + 1)], psH[:],
                                routeX[:, 512 * th:512 * (th + 1)], ALU.mult).then_inc(vhw, 1)
            for u in range(64):
                ot, th = divmod(u, 2)
                v.wait_ge(sgl, u + 1)
                v.wait_ge(ss1, u + 1)
                v.wait_ge(lx1[u % 2], 16 * (u // 2 + 1))
                if u >= 1:
                    v.wait_ge(vvo, 2 * u)
                v.tensor_tensor(tt2[:], g1[th][:], s1[th][:], ALU.mult).then_inc(vvo, 1)
                v.wait_ge(vvo, 2 * u + 1)
                if u >= 2:
                    v.wait_ge(sow2[u % 2], 16 * (u // 2))
                v.tensor_tensor(ob[u % 2][:], tt2[:], x1t[u % 2][:],
                                ALU.add).then_inc(vvo, 1)
    return nc


def kernel(L0_attn_x, L0_attn_wq, L0_attn_wk, L0_attn_wv, L0_attn_wo,
           L0_ffn_gate_qweight, L0_ffn_gate_scales, L0_ffn_gate_qzeros,
           L0_ffn_up_qweight, L0_ffn_up_scales, L0_ffn_up_qzeros,
           L0_ffn_down_qweight, L0_ffn_down_scales, L0_ffn_down_qzeros,
           L1_attn_k_cache, L1_attn_v_cache, L1_attn_Wq, L1_attn_Wo,
           L1_attn_block_table, L1_ffn_W_base, L1_ffn_b_base, L1_ffn_W_gate,
           L1_ffn_A, L1_ffn_B):
    f32 = np.float32
    cores = list(range(NCORES))
    x2 = np.asarray(L0_attn_x, dtype=f32).reshape(8192, D)
    eye = np.eye(128, dtype=f32)

    mapsA = host_prep0_inputs(x2, np.asarray(L0_attn_wq, f32), np.asarray(L0_attn_wk, f32),
                              np.asarray(L0_attn_wv, f32))
    resA = run_bass_kernel_spmd(build_prep0(), mapsA, core_ids=cores).results
    xnT_c = [np.asarray(resA[c]["xnT"]) for c in range(NCORES)]

    mapsB = host_attn0_inputs([np.asarray(resA[c]["qT"]) for c in range(NCORES)],
                              [np.asarray(resA[c]["kT"]) for c in range(NCORES)],
                              [np.asarray(resA[c]["v"]) for c in range(NCORES)])
    resB = run_bass_kernel_spmd(build_attn0(), mapsB, core_ids=cores).results
    ctx_b = host_attn0_ctx(resB)

    mapsC, lp_gate, lp_down = host_dequant_inputs(
        np.asarray(L0_ffn_gate_qweight), np.asarray(L0_ffn_gate_scales, f32), np.asarray(L0_ffn_gate_qzeros),
        np.asarray(L0_ffn_up_qweight), np.asarray(L0_ffn_up_scales, f32), np.asarray(L0_ffn_up_qzeros),
        np.asarray(L0_ffn_down_qweight), np.asarray(L0_ffn_down_scales, f32), np.asarray(L0_ffn_down_qzeros))
    resC = run_bass_kernel_spmd(build_dequant(), mapsC, core_ids=cores).results
    Wg_full = np.concatenate([np.asarray(resC[c]["Wg"]) for c in range(NCORES)], axis=1)
    Wu_full = np.concatenate([np.asarray(resC[c]["Wu"]) for c in range(NCORES)], axis=1)
    Wd_full = np.concatenate([np.asarray(resC[c]["Wd"]) for c in range(NCORES)], axis=1)

    wgs = cc(Wg_full.reshape(D, 86, 128).transpose(1, 0, 2))
    wus = cc(Wu_full.reshape(D, 86, 128).transpose(1, 0, 2))
    mapsD = [dict(xnT=xnT_c[c], wgs=wgs, wus=wus) for c in range(NCORES)]
    resD = run_bass_kernel_spmd(build_ffn0(), mapsD, core_ids=cores).results

    wds = cc(Wd_full.reshape(11008, 32, 128).transpose(1, 0, 2))
    wo_perm = np.asarray(L0_attn_wo, f32)[:, lp_down]
    wosE = cc(wo_perm.reshape(D, 32, 128).transpose(1, 0, 2))
    one_r = np.ones((1, 128), f32)
    one_c = np.ones((128, 1), f32)
    mapsE = []
    for c in range(NCORES):
        b, m = c // 4, c % 4
        ctxT = cc(ctx_b[b][1024 * m:1024 * (m + 1), :].T)
        xTp = cc(x2[1024 * c:1024 * (c + 1)].T[lp_down])
        mapsE.append(dict(hT=np.asarray(resD[c]["hT"]), wds=wds, cxT=ctxT,
                          wos=wosE, xTp=xTp, one=one_r, onec=one_c))
    resE = run_bass_kernel_spmd(build_downwo(), mapsE, core_ids=cores).results
    inv_perm = np.argsort(lp_down)
    x1T_c = [cc(np.asarray(resE[c]["x1T"])[inv_perm]) for c in range(NCORES)]
    xn1T_c = [cc(np.asarray(resE[c]["xn1T"])[inv_perm]) for c in range(NCORES)]

    wqsF = cc(np.asarray(L1_attn_Wq, f32).T.reshape(D, 32, 128).transpose(1, 0, 2))
    kcF = np.asarray(L1_attn_k_cache, f32)
    vcF = np.asarray(L1_attn_v_cache, f32)
    bti = cc(np.asarray(L1_attn_block_table, np.int32).reshape(2, 128).T.copy())
    inv = 10000.0 ** (-np.arange(64) / 64.0)
    sc128 = f32(1.0 / np.sqrt(HD))
    mapsF = []
    for c in range(NCORES):
        pos = np.arange(1024 * (c % 4), 1024 * (c % 4) + 1024, dtype=np.float64)
        ang = pos[:, None] * inv[None, :]
        Ct = cc(np.concatenate([np.cos(ang), np.cos(ang)], 1).T.astype(f32)) * sc128
        St = cc(np.concatenate([-np.sin(ang), np.sin(ang)], 1).T.astype(f32)) * sc128
        mapsF.append(dict(xn1T=xn1T_c[c], wqs=wqsF, c1=Ct, s1=St,
                          kc=kcF, vc=vcF, bti=bti, eye=eye))
    resF = run_bass_kernel_spmd(build_attn1(), mapsF, core_ids=cores).results

    wosG = cc(np.asarray(L1_attn_Wo, f32).T.reshape(D, 32, 128).transpose(1, 0, 2))
    wbsG = cc(np.asarray(L1_ffn_W_base, f32).T.reshape(D, 32, 128).transpose(1, 0, 2))
    bbG = cc(np.asarray(L1_ffn_b_base, f32).reshape(32, 128).T)
    wg8G = cc(np.asarray(L1_ffn_W_gate, f32).T)
    atG = cc(np.asarray(L1_ffn_A, f32).transpose(2, 0, 1).reshape(D, 128))
    bt8G = cc(np.asarray(L1_ffn_B, f32).transpose(0, 2, 1).reshape(128, D))
    e16 = np.zeros((8, 128), f32)
    for e in range(8):
        e16[e, 16 * e:16 * (e + 1)] = 1.0
    mapsG = []
    for c in range(NCORES):
        arr = np.asarray(resF[c]["ctx1"])
        a6 = arr.reshape(8, 128, 4, 2, 4, 128)
        ctx1 = a6.transpose(3, 4, 1, 0, 2, 5).reshape(1024, D)
        mapsG.append(dict(xn1T=xn1T_c[c], cx1T=cc(ctx1.T), x1T=x1T_c[c],
                          wos=wosG, wbs=wbsG, bb=bbG, wg8=wg8G, at=atG,
                          bt8=bt8G, e16=e16, eye=eye))
    resG = run_bass_kernel_spmd(build_final(), mapsG, core_ids=cores).results

    out = np.empty((8192, D), dtype=f32)
    for c in range(NCORES):
        out[1024 * c:1024 * (c + 1)] = np.asarray(resG[c]["outT"]).T
    return out.reshape(2, 4096, 4096)
